# version 04
# baseline (speedup 1.0000x reference)
import numpy as np
import ml_dtypes
from contextlib import ExitStack

import concourse.bass as bass
import concourse.mybir as mybir
from concourse.bass_utils import run_bass_kernel_spmd

F32 = mybir.dt.float32
BF16 = mybir.dt.bfloat16
AF = mybir.ActivationFunctionType
ALU = mybir.AluOpType
AX = mybir.AxisListType

S = 2048
D = 2048
ST = S // 128
DT = D // 128
NEG = -1.0e30
WDTR_STRIDES = [1, 2, 4, 8, 16, 32]

OFF_BASE = 16512
OFF_BIG0 = OFF_BASE
OFF_BIG1 = OFF_BASE + 65536
OFF_BIG2 = OFF_BASE + 131072
OFF_RS = OFF_BASE + 196608
SBUF_LIMIT = 229376

L0_PHASES = ["ln0", "l0q", "l0v", "l0k", "l0sc", "l0av", "l0ao",
             "l0gate", "l0k1", "l0selk", "l0kshT", "l0v1", "l0selv",
             "l0q1", "l0fat"]
L1_PHASES = ["ln1", "l1v", "l1at", "l1o", "wdtr", "l1f1", "l1f2a", "l1f2b",
             "l1f2c", "l1f2d"]
ALL_PHASES = L0_PHASES + L1_PHASES
GROUPS = [ALL_PHASES]
EXPOSE = []
FINAL_OUT = "hbm_out"
N_CORES = 8

ENGS = ["sync", "gpsimd", "tensor", "vector", "scalar"]


def _hadamard(n):
    H = np.array([[1.0]])
    while H.shape[0] < n:
        H = np.block([[H, H], [H, -H]])
    return H


def _consts():
    c = {}
    c["ident"] = np.eye(128, dtype=ml_dtypes.bfloat16)
    c["hq"] = (_hadamard(128) / 64.0).astype(ml_dtypes.bfloat16)
    r = np.arange(128)
    c["mask_causal"] = np.where(r[None, :] <= r[:, None], 0.0, NEG).astype(np.float32)
    c["mask_upper"] = np.where(r[None, :] > r[:, None], 0.0, NEG).astype(np.float32)
    half = 64
    theta = 1.0 / (10000.0 ** (np.arange(half, dtype=np.float64) / half))
    freqs = np.arange(S, dtype=np.float64)[:, None] * theta[None, :]
    cos = np.cos(freqs).astype(np.float32)
    sin = np.sin(freqs).astype(np.float32)
    c["cosT"] = np.ascontiguousarray(np.concatenate([cos.T, cos.T], axis=0))
    c["sinT"] = np.ascontiguousarray(np.concatenate([sin.T, sin.T], axis=0))
    c["onesr"] = np.ones((1, 128), dtype=ml_dtypes.bfloat16)
    R = np.zeros((128, 128), dtype=np.float32)
    for i_ in range(64):
        R[i_, 64 + i_] = -1.0
        R[64 + i_, i_] = 1.0
    c["rotT"] = np.ascontiguousarray(R.T).astype(ml_dtypes.bfloat16)
    misc = np.zeros((128, 4), dtype=np.float32)
    misc[:, 0] = 1e-5
    misc[:, 1] = 1e-6
    misc[:, 2] = 1.0
    c["misc"] = misc
    return c


class K:
    def __init__(self, nc, stack, block):
        self.nc = nc
        self.block = block
        self.sem = {}
        self.cnt = {}
        for name in ["sdw", "sdl", "sds", "pe", "dve", "act", "pool",
                     "wd0", "wd1", "wd2", "wd3", "wd4"]:
            self.sem[name] = stack.enter_context(nc.semaphore(name))
            self.cnt[name] = 0
        self._uid = 0
        self.guard = {}

    def uid(self):
        self._uid += 1
        return self._uid

    def inc(self, instr, name, n=None):
        if n is None:
            n = 16 if (name.startswith("sd") or name.startswith("wd")) else 1
        instr.then_inc(self.sem[name], n)
        self.cnt[name] += n
        return self.cnt[name]

    def wait(self, eng, name, val):
        if val and val > 0:
            eng.wait_ge(self.sem[name], val)

    def snap(self):
        return dict(self.cnt)

    def wait_snap(self, eng, snap):
        for name, val in snap.items():
            if val > 0:
                eng.wait_ge(self.sem[name], val)

    def sb(self, shape, dtype, off, name=None):
        name = name or f"t{self.uid()}"
        return self.nc.alloc_sbuf_tensor_at(name, shape, dtype, offset=off)

    def guard_wait(self, eng, slot):
        g = self.guard.get(slot)
        if g is not None:
            self.wait(eng, g[0], g[1])

    def guard_set(self, slot, semname):
        self.guard[slot] = (semname, self.cnt[semname])

    def sec(self, engine, fn):
        getattr(self.block, engine)(fn)

    def prologue(self, engines=ENGS):
        bar = self.snap()
        self.guard.clear()
        for e in engines:
            def f(eng, bar=bar):
                self.wait_snap(eng, bar)
            self.sec(e, f)
        return bar


def mm_group(k, te, ps_slice, pairs):
    ins = None
    for i, (lhsT, rhs) in enumerate(pairs):
        ins = te.matmul(ps_slice, lhsT, rhs, start=(i == 0), stop=(i == len(pairs) - 1))
    return k.inc(ins, "pe")


def evac(k, eng, engname, dst, src, wait=None):
    if wait:
        k.wait(eng, wait[0], wait[1])
    if engname == "dve":
        ins = eng.tensor_copy(dst, src)
        return k.inc(ins, "dve")
    else:
        ins = eng.copy(dst, src)
        return k.inc(ins, "act")


def transpose_block(k, sb, jobs, slots, slot_tag):
    n = len(slots)
    jobs = [j if len(j) == 4 else (j[0], j[1], j[2], None) for j in jobs]
    for r0 in range(0, len(jobs), n):
        rnd = jobs[r0:r0 + n]

        def f_tr(te, rnd=rnd):
            for i, (src, dst, waits, sc_) in enumerate(rnd):
                k.guard_wait(te, (slot_tag, i))
                for w in waits:
                    k.wait(te, w[0], w[1])
                ins = te.transpose(slots[i], src, sb["ident"][:])
                rnd[i] = (src, dst, waits, sc_, k.inc(ins, "pe"))
        k.sec("tensor", f_tr)

        def f_ev_v(v, rnd=rnd):
            for i, job in enumerate(rnd):
                if i % 2 == 0:
                    k.wait(v, "pe", job[4])
                    if job[3] is None:
                        ins = v.tensor_copy(job[1], slots[i])
                    else:
                        ins = v.tensor_scalar(job[1], slots[i], job[3], None, ALU.mult)
                    k.inc(ins, "dve")
                    k.guard_set((slot_tag, i), "dve")
        k.sec("vector", f_ev_v)

        def f_ev_s(sc, rnd=rnd):
            for i, job in enumerate(rnd):
                if i % 2 == 1:
                    k.wait(sc, "pe", job[4])
                    if job[3] is None:
                        ins = sc.copy(job[1], slots[i])
                    else:
                        ins = sc.activation(job[1], slots[i], AF.Copy, scale=job[3])
                    k.inc(ins, "act")
                    k.guard_set((slot_tag, i), "act")
        k.sec("scalar", f_ev_s)



def ph_ln0(k, io, sb, pstack):
    k.prologue()
    x = io["x"]
    xf = [k.sb([128, 2048], F32, OFF_BIG2 + i * 8192, f"ln0xf{i}") for i in range(2)]
    xc = [k.sb([128, 2048], F32, OFF_BIG2 + 16384 + i * 8192, f"ln0xc{i}") for i in range(2)]
    xq = k.sb([128, 2048], F32, OFF_BIG2 + 32768, "ln0xq")
    xnb = [k.sb([128, 2048], BF16, OFF_BIG2 + 40960 + i * 4096, f"ln0xn{i}") for i in range(2)]
    cols = k.sb([128, 5 * ST], F32, OFF_RS, "ln0cols")
    xnT = sb["xnT"]
    pts = [pstack.enter_context(k.nc.psum_tensor(f"ln0pt{i}", [128, 1024], BF16))
           for i in range(8)]
    slots = [p[:, 0:128] for p in pts]
    m = {}

    for t in range(ST):
        b = t % 2

        def f_load(s, t=t, b=b):
            if t >= 2:
                k.wait(s, "dve", m[("xc", t - 2)])
            m[("ld", t)] = k.inc(s.dma_start(xf[b][:], x[t * 128:(t + 1) * 128, :]), "sdl")
        k.sec("sync", f_load)

        def f_sum(v, t=t, b=b):
            k.wait(v, "sdl", m[("ld", t)])
            ins = v.tensor_reduce(cols[:, t:t + 1], xf[b][:], axis=AX.X, op=ALU.add)
            m[("sum", t)] = k.inc(ins, "dve")
        k.sec("vector", f_sum)

        def f_nm(sc, t=t):
            k.wait(sc, "dve", m[("sum", t)])
            ins = sc.activation(cols[:, ST + t:ST + t + 1], cols[:, t:t + 1], AF.Copy,
                                scale=-1.0 / D)
            m[("nm", t)] = k.inc(ins, "act")
        k.sec("scalar", f_nm)

        def f_xc(v, t=t, b=b):
            k.wait(v, "act", m[("nm", t)])
            if t >= 2:
                k.wait(v, "act", m[("sq", t - 2)])
            ins = v.tensor_scalar_add(xc[b][:], xf[b][:], cols[:, ST + t:ST + t + 1])
            m[("xc", t)] = k.inc(ins, "dve")
        k.sec("vector", f_xc)

        def f_sq(sc, t=t, b=b):
            k.wait(sc, "dve", m[("xc", t)])
            ins = sc.activation(xq[:], xc[b][:], AF.Square,
                                accum_out=cols[:, 2 * ST + t:2 * ST + t + 1])
            v1 = k.inc(ins, "act")
            k.wait(sc, "act", v1)
            ins = sc.activation(cols[:, 3 * ST + t:3 * ST + t + 1],
                                cols[:, 2 * ST + t:2 * ST + t + 1], AF.Sqrt,
                                scale=1.0 / D, bias=sb["misc"][:, 0:1])
            m[("sq", t)] = k.inc(ins, "act")
        k.sec("scalar", f_sq)

        def f_xn(v, t=t, b=b):
            k.wait(v, "act", m[("sq", t)])
            ins = v.reciprocal(cols[:, 4 * ST + t:4 * ST + t + 1],
                               cols[:, 3 * ST + t:3 * ST + t + 1])
            v1 = k.inc(ins, "dve")
            k.wait(v, "dve", v1)
            if t >= 2:
                k.wait(v, "pe", m[("trd", t - 2)])
            ins = v.tensor_scalar(xnb[b][:], xc[b][:], cols[:, 4 * ST + t:4 * ST + t + 1],
                                  None, ALU.mult)
            m[("xn", t)] = k.inc(ins, "dve")
        k.sec("vector", f_xn)

        jobs = []
        for j in range(DT):
            jobs.append((xnb[b][:, j * 128:(j + 1) * 128],
                         xnT[:, j * 2048 + t * 128: j * 2048 + (t + 1) * 128],
                         [("dve", m[("xn", t)])]))
        transpose_block(k, sb, jobs, slots, "ln0pt")
        m[("trd", t)] = k.cnt["pe"]

    end = k.snap()

    def f_spill(s):
        k.wait_snap(s, end)
        k.inc(s.dma_start(io["hbm_xnT"], sb["xnT"][:]), "sds")
    k.sec("sync", f_spill)



def ph_proj_se(k, io, sb, pstack, wname, outbuf, spill=None, col0=0, inbuf=None,
               reload_in=None, tag=None):
    k.prologue()
    tag = tag or f"{wname}{col0}"
    xnT = inbuf if inbuf is not None else sb["xnT"]
    w = io[wname]
    wch = [k.sb([128, 16 * 512], BF16, OFF_BIG1 + i * 16384, f"{tag}c{i}") for i in range(2)]
    psA = pstack.enter_context(k.nc.psum_tensor(f"{tag}ps", [128, 2048], F32))
    m = {}

    if reload_in is not None:
        def f_rl(s):
            m["rl"] = k.inc(s.dma_start(xnT[:], io[reload_in]), "sdl")
        k.sec("sync", f_rl)

    for ech in range(4):
        def f_w(g, ech=ech):
            if ech >= 2:
                k.wait(g, "pe", m[("mmd", ech - 2)])
            src = w[:, col0 + ech * 512: col0 + (ech + 1) * 512].rearrange(
                "(dt p) c -> p dt c", p=128)
            dst = wch[ech % 2][:].rearrange("p (dt c) -> p dt c", dt=DT)
            m[("w", ech)] = k.inc(g.dma_start(dst, src), "sdw")
        k.sec("gpsimd", f_w)

        for st in range(ST):
            def f_mm(te, ech=ech, st=st):
                k.wait(te, "sdw", m[("w", ech)])
                if reload_in is not None:
                    k.wait(te, "sdl", m["rl"])
                k.guard_wait(te, ("psA", st % 4))
                ps = psA[:, (st % 4) * 512:(st % 4 + 1) * 512]
                pairs = [(xnT[:, d * 2048 + st * 128: d * 2048 + (st + 1) * 128],
                          wch[ech % 2][:, d * 512:(d + 1) * 512]) for d in range(DT)]
                m[("mm", ech, st)] = mm_group(k, te, ps, pairs)
                if st == ST - 1:
                    m[("mmd", ech)] = m[("mm", ech, st)]
            k.sec("tensor", f_mm)

            engname = "dve" if st % 2 == 0 else "act"

            def f_ev(e, ech=ech, st=st, engname=engname):
                ps = psA[:, (st % 4) * 512:(st % 4 + 1) * 512]
                evac(k, e, engname,
                     outbuf[:, st * 2048 + ech * 512: st * 2048 + (ech + 1) * 512],
                     ps, wait=("pe", m[("mm", ech, st)]))
                k.guard_set(("psA", st % 4), engname)
            k.sec("vector" if engname == "dve" else "scalar", f_ev)

    if spill is not None:
        end = k.snap()

        def f_spill(s):
            k.wait_snap(s, end)
            k.inc(s.dma_start(io[spill], outbuf[:]), "sds")
        k.sec("sync", f_spill)



def ph_proj_fm(k, io, sb, pstack, wname, outbuf, spill=None, col0=0, inbuf=None,
               tag=None, woff=None):
    k.prologue()
    tag = tag or f"fm{wname}{col0}"
    xnT = inbuf if inbuf is not None else sb["xnT"]
    w = io[wname]
    if woff is None:
        woff = OFF_BIG1
    wch = [k.sb([128, 16 * 128], BF16, woff + i * 4096, f"{tag}c{i}") for i in range(2)]
    psA = pstack.enter_context(k.nc.psum_tensor(f"{tag}ps", [128, 2048], F32))
    m = {}

    for ft in range(DT):
        def f_w(g, ft=ft):
            if ft >= 2:
                k.wait(g, "pe", m[("mmd", ft - 2)])
            src = w[:, col0 + ft * 128: col0 + (ft + 1) * 128].rearrange(
                "(dt p) c -> p dt c", p=128)
            dst = wch[ft % 2][:].rearrange("p (dt c) -> p dt c", dt=DT)
            m[("w", ft)] = k.inc(g.dma_start(dst, src), "sdw")
        k.sec("gpsimd", f_w)

        for ch in range(4):
            def f_mm(te, ft=ft, ch=ch):
                k.wait(te, "sdw", m[("w", ft)])
                k.guard_wait(te, ("psA", ch))
                ps = psA[:, ch * 512:(ch + 1) * 512]
                pairs = [(wch[ft % 2][:, d * 128:(d + 1) * 128],
                          xnT[:, d * 2048 + ch * 512: d * 2048 + (ch + 1) * 512])
                         for d in range(DT)]
                m[("mm", ft, ch)] = mm_group(k, te, ps, pairs)
                if ch == 3:
                    m[("mmd", ft)] = m[("mm", ft, ch)]
            k.sec("tensor", f_mm)

            engname = "dve" if ch % 2 == 0 else "act"

            def f_ev(e, ft=ft, ch=ch, engname=engname):
                ps = psA[:, ch * 512:(ch + 1) * 512]
                evac(k, e, engname,
                     outbuf[:, ft * 2048 + ch * 512: ft * 2048 + (ch + 1) * 512],
                     ps, wait=("pe", m[("mm", ft, ch)]))
                k.guard_set(("psA", ch), engname)
            k.sec("vector" if engname == "dve" else "scalar", f_ev)

    if spill is not None:
        end = k.snap()

        def f_spill(s):
            k.wait_snap(s, end)
            k.inc(s.dma_start(io[spill], outbuf[:]), "sds")
        k.sec("sync", f_spill)



def ph_l0_scores(k, io, sb, pstack, qbuf, kbuf, ptbuf):
    k.prologue()
    m = {}
    cols = k.sb([128, 3 * ST], F32, OFF_RS, "scc")
    pbf = [k.sb([128, 2048], BF16, OFF_RS + 256 + i * 4096, f"scp{i}") for i in range(2)]
    psA = pstack.enter_context(k.nc.psum_tensor("scps", [128, 2048], F32))
    pts = [pstack.enter_context(k.nc.psum_tensor(f"scpt{i}", [128, 1024], BF16))
           for i in range(4)]
    slots = [p[:, 0:128] for p in pts]

    def f_q(s):
        m["q"] = k.inc(s.dma_start(qbuf[:], io["hbm_q"]), "sdl")
    k.sec("sync", f_q)

    for et in range(ST):
        def f_mm(te, et=et):
            if et == 0:
                k.wait(te, "sdl", m["q"])
            else:
                k.wait(te, "act", m[("exp", et - 1)])
            for ch in range(4):
                ps = psA[:, ch * 512:(ch + 1) * 512]
                pairs = [(qbuf[:, stt * 2048 + et * 128: stt * 2048 + (et + 1) * 128],
                          kbuf[:, stt * 2048 + ch * 512: stt * 2048 + (ch + 1) * 512])
                         for stt in range(ST)]
                m[("mm", et, ch)] = mm_group(k, te, ps, pairs)
        k.sec("tensor", f_mm)

        def f_nm(v, et=et):
            k.wait(v, "pe", m[("mm", et, 3)])
            ins = v.tensor_reduce(cols[:, et:et + 1], psA[:], axis=AX.X, op=ALU.max,
                                  negate=True)
            m[("nm", et)] = k.inc(ins, "dve")
        k.sec("vector", f_nm)

        def f_exp(sc, et=et):
            k.wait(sc, "dve", m[("nm", et)])
            if et >= 2:
                k.wait(sc, "pe", m[("trd", et - 2)])
            b = et % 2
            ins = sc.activation(cols[:, 2 * ST + et:2 * ST + et + 1],
                                cols[:, et:et + 1], AF.Copy, scale=0.125)
            v1 = k.inc(ins, "act")
            k.wait(sc, "act", v1)
            ins = sc.activation(pbf[b][:], psA[:], AF.Exp,
                                bias=cols[:, 2 * ST + et:2 * ST + et + 1], scale=0.125,
                                accum_out=cols[:, ST + et:ST + et + 1])
            m[("exp", et)] = k.inc(ins, "act")
        k.sec("scalar", f_exp)

        def f_div(v, et=et):
            k.wait(v, "act", m[("exp", et)])
            ins = v.reciprocal(cols[:, 2 * ST + et:2 * ST + et + 1],
                               cols[:, ST + et:ST + et + 1])
            v1 = k.inc(ins, "dve")
            k.wait(v, "dve", v1)
            b = et % 2
            ins = v.tensor_scalar(pbf[b][:], pbf[b][:],
                                  cols[:, 2 * ST + et:2 * ST + et + 1], None, ALU.mult)
            m[("div", et)] = k.inc(ins, "dve")
        k.sec("vector", f_div)

        b = et % 2
        jobs = []
        for j in range(DT):
            jobs.append((pbf[b][:, j * 128:(j + 1) * 128],
                         ptbuf[:, j * 2048 + et * 128: j * 2048 + (et + 1) * 128],
                         [("dve", m[("div", et)])]))
        transpose_block(k, sb, jobs, slots, "scpt")
        m[("trd", et)] = k.cnt["pe"]



def ph_l0_av(k, io, sb, pstack, ptbuf, vbuf, outbuf):
    k.prologue()
    m = {}
    psA = pstack.enter_context(k.nc.psum_tensor("avps", [128, 2048], F32))

    def f_v(s):
        m["v"] = k.inc(s.dma_start(vbuf[:], io["hbm_v0T"]), "sdl")
    k.sec("sync", f_v)

    for et in range(ST):
        for ch in range(4):
            def f_mm(te, et=et, ch=ch):
                if et == 0 and ch == 0:
                    k.wait(te, "sdl", m["v"])
                k.guard_wait(te, ("psA", ch))
                ps = psA[:, ch * 512:(ch + 1) * 512]
                pairs = [(ptbuf[:, f * 2048 + et * 128: f * 2048 + (et + 1) * 128],
                          vbuf[:, f * 2048 + ch * 512: f * 2048 + (ch + 1) * 512])
                         for f in range(DT)]
                m[("mm", et, ch)] = mm_group(k, te, ps, pairs)
            k.sec("tensor", f_mm)

            engname = "dve" if ch % 2 == 0 else "act"

            def f_ev(e, et=et, ch=ch, engname=engname):
                ps = psA[:, ch * 512:(ch + 1) * 512]
                evac(k, e, engname,
                     outbuf[:, et * 2048 + ch * 512: et * 2048 + (ch + 1) * 512],
                     ps, wait=("pe", m[("mm", et, ch)]))
                k.guard_set(("psA", ch), engname)
            k.sec("vector" if engname == "dve" else "scalar", f_ev)



def ph_l0_aout(k, io, sb, pstack, aoT, wobuf):
    k.prologue()
    m = {}
    xf = [k.sb([128, 2048], F32, OFF_BIG1 + i * 8192, f"aoxf{i}") for i in range(2)]
    x1f = [k.sb([128, 2048], F32, OFF_BIG1 + 16384 + i * 8192, f"aox1{i}") for i in range(2)]
    psA = pstack.enter_context(k.nc.psum_tensor("aops", [128, 2048], F32))

    def f_wo(g):
        src = io["l0wo"].rearrange("(dt p) c -> p dt c", p=128)
        dst = wobuf[:].rearrange("p (dt c) -> p dt c", dt=DT)
        m["wo"] = k.inc(g.dma_start(dst, src), "sdw")
    k.sec("gpsimd", f_wo)

    for st in range(ST):
        def f_x(s, st=st):
            if st >= 2:
                k.wait(s, "dve", m[("add", st - 2, 3)])
            m[("x", st)] = k.inc(
                s.dma_start(xf[st % 2][:], io["x"][st * 128:(st + 1) * 128, :]), "sdl")
        k.sec("sync", f_x)

        def f_mm(te, st=st):
            if st == 0:
                k.wait(te, "sdw", m["wo"])
            for ch in range(4):
                k.guard_wait(te, ("psA", ch))
                ps = psA[:, ch * 512:(ch + 1) * 512]
                pairs = [(aoT[:, e * 2048 + st * 128: e * 2048 + (st + 1) * 128],
                          wobuf[:, e * 2048 + ch * 512: e * 2048 + (ch + 1) * 512])
                         for e in range(DT)]
                m[("mm", st, ch)] = mm_group(k, te, ps, pairs)
        k.sec("tensor", f_mm)

        def f_add(v, st=st):
            b = st % 2
            k.wait(v, "sdl", m[("x", st)])
            if st >= 2:
                k.wait(v, "sds", m[("st", st - 2)])
            for ch in range(4):
                k.wait(v, "pe", m[("mm", st, ch)])
                ps = psA[:, ch * 512:(ch + 1) * 512]
                ins = v.tensor_add(x1f[b][:, ch * 512:(ch + 1) * 512], ps,
                                   xf[b][:, ch * 512:(ch + 1) * 512])
                m[("add", st, ch)] = k.inc(ins, "dve")
                k.guard_set(("psA", ch), "dve")
        k.sec("vector", f_add)

        def f_st(s, st=st):
            k.wait(s, "dve", m[("add", st, 3)])
            ins = s.dma_start(io["hbm_x1"][st * 128:(st + 1) * 128, :], x1f[st % 2][:])
            m[("st", st)] = k.inc(ins, "sds")
        k.sec("sync", f_st)





def ph_l0gate(k, io, sb, pstack):
    k.prologue()
    xnT = sb["xnT"]
    wg = k.sb([128, 16 * 16], BF16, OFF_RS + 9728, "wgsb")
    cols = k.sb([128, 2 * ST], F32, OFF_RS + 10240, "gcols")
    ind = sb["ind"]
    rz = sb["rz"]
    esc = k.sb([128, 16], F32, OFF_RS + 11264, "gesc")
    psG = pstack.enter_context(k.nc.psum_tensor("gps", [128, 1024], F32))
    m = {}

    def f_rl(s):
        m["rl"] = k.inc(s.dma_start(xnT[:], io["hbm_xnT"]), "sdl")
    k.sec("sync", f_rl)

    def f_w(g):
        src = io["wgT"].rearrange("(dt p) c -> p dt c", p=128)
        dst = wg[:].rearrange("p (dt c) -> p dt c", dt=DT)
        m["w"] = k.inc(g.dma_start(dst, src), "sdw")
    k.sec("gpsimd", f_w)

    for t in range(ST):
        def f_mm(te, t=t):
            if t == 0:
                k.wait(te, "sdw", m["w"])
                k.wait(te, "sdl", m["rl"])
            k.guard_wait(te, ("g", t % 2))
            ps = psG[:, (t % 2) * 512:(t % 2) * 512 + 16]
            pairs = [(xnT[:, d * 2048 + t * 128: d * 2048 + (t + 1) * 128],
                      wg[:, d * 16:(d + 1) * 16]) for d in range(DT)]
            m[("mm", t)] = mm_group(k, te, ps, pairs)
        k.sec("tensor", f_mm)

        def f_nm(v, t=t):
            k.wait(v, "pe", m[("mm", t)])
            ps = psG[:, (t % 2) * 512:(t % 2) * 512 + 16]
            ins = v.tensor_reduce(cols[:, t:t + 1], ps, axis=AX.X, op=ALU.max,
                                  negate=True)
            v1 = k.inc(ins, "dve")
            k.wait(v, "dve", v1)
            ins = v.tensor_scalar(ind[:, t * 16:(t + 1) * 16], ps,
                                  cols[:, t:t + 1], sb["misc"][:, 3:4],
                                  ALU.add, ALU.is_equal)
            m[("ind", t)] = k.inc(ins, "dve")
        k.sec("vector", f_nm)

        def f_exp(sc, t=t):
            k.wait(sc, "dve", m[("ind", t)])
            ps = psG[:, (t % 2) * 512:(t % 2) * 512 + 16]
            ins = sc.activation(esc[:], ps, AF.Exp, bias=cols[:, t:t + 1],
                                accum_out=cols[:, ST + t:ST + t + 1])
            m[("exp", t)] = k.inc(ins, "act")
            k.guard_set(("g", t % 2), "act")
        k.sec("scalar", f_exp)

        def f_rz(v, t=t):
            k.wait(v, "act", m[("exp", t)])
            ins = v.reciprocal(rz[:, t:t + 1], cols[:, ST + t:ST + t + 1])
            m[("rz", t)] = k.inc(ins, "dve")
        k.sec("vector", f_rz)



def ph_l0sel(k, io, sb, pstack, srcbuf, dst_name, use_rz):
    k.prologue(["vector", "scalar", "tensor"])
    ind = sb["ind"]
    rz = sb["rz"]
    acc = k.sb([128, 128], F32, OFF_RS + 10240, "selacc")
    dst = sb[dst_name]
    m = {}
    for t in range(ST):
        def f_sel(v, t=t):
            last = None
            for h in range(16):
                colh = ind[:, t * 16 + h: t * 16 + h + 1]
                srcs = srcbuf[:, t * 2048 + h * 128: t * 2048 + (h + 1) * 128]
                if h == 0:
                    ins = v.tensor_scalar(acc[:], srcs, colh, None, ALU.mult)
                else:
                    k.wait(v, "dve", last)
                    ins = v.scalar_tensor_tensor(acc[:], srcs, colh, acc[:],
                                                 ALU.mult, ALU.add)
                last = k.inc(ins, "dve")
            k.wait(v, "dve", last)
            if use_rz:
                ins = v.tensor_scalar(dst[:, t * 128:(t + 1) * 128], acc[:],
                                      rz[:, t:t + 1], None, ALU.mult)
            else:
                ins = v.tensor_copy(dst[:, t * 128:(t + 1) * 128], acc[:])
            m[("sel", t)] = k.inc(ins, "dve")
        k.sec("vector", f_sel)
    return m


def ph_l0ksh_T(k, io, sb, pstack):
    k.prologue(["tensor", "vector", "scalar"])
    pts = [pstack.enter_context(k.nc.psum_tensor(f"kst{i}", [128, 1024], BF16))
           for i in range(4)]
    slots = [p[:, 0:128] for p in pts]
    jobs = []
    for t in range(ST):
        jobs.append((sb["k_sh"][:, t * 128:(t + 1) * 128],
                     sb["k_shT"][:, t * 128:(t + 1) * 128], []))
    transpose_block(k, sb, jobs, slots, "kst")



def ph_l0fat(k, io, sb, pstack, q1T):
    k.prologue()
    CS = 0.08838834764831845
    pb = [k.sb([128, 2048], BF16, OFF_BIG2 + i * 4096, f"fapb{i}") for i in range(2)]
    pt = [k.sb([128, 2048], BF16, OFF_BIG2 + 8192 + i * 4096, f"fapt{i}") for i in range(2)]
    Ft = k.sb([128, 2048], F32, OFF_BIG2 + 16384, "faF")
    xr = [k.sb([128, 2048], F32, OFF_BIG2 + 24576 + i * 8192, f"faxr{i}") for i in range(2)]
    cols = k.sb([128, 4], F32, OFF_RS + 10240, "facols")
    k_shT = sb["k_shT"]
    v_sh = sb["v_sh"]
    psA = pstack.enter_context(k.nc.psum_tensor("faps", [128, 2048], F32))
    psY = pstack.enter_context(k.nc.psum_tensor("fay", [128, 512], F32))
    pts = [pstack.enter_context(k.nc.psum_tensor(f"fat{i}", [128, 1024], BF16))
           for i in range(3)]
    slots = [p[:, 0:128] for p in pts]
    m = {}

    for i in range(ST):
        def f_x(s, i=i):
            if i >= 2:
                k.wait(s, "sds", m[("st", i - 2)])
            m[("x", i)] = k.inc(
                s.dma_start(xr[i % 2][:], io["hbm_x1"][i * 128:(i + 1) * 128, :]), "sdl")
        k.sec("sync", f_x)

        nk = i + 1
        kw = nk * 128
        for h in range(16):
            def f_mm(te, i=i, h=h, kw=kw):
                if h > 0:
                    k.wait(te, "act", m[("exp", i, h - 1)])
                elif i > 0:
                    k.wait(te, "act", m[("exp", i - 1, 15)])
                for c0 in range(0, kw, 512):
                    w = min(512, kw - c0)
                    ps = psA[:, c0:c0 + w]
                    ins = te.matmul(ps, q1T[:, h * 2048 + i * 128: h * 2048 + (i + 1) * 128],
                                    k_shT[:, c0:c0 + w], start=True, stop=True)
                m[("mm", i, h)] = k.inc(ins, "pe")
            k.sec("tensor", f_mm)

            def f_sm(v, i=i, h=h, kw=kw):
                k.wait(v, "pe", m[("mm", i, h)])
                ins = v.tensor_add(psA[:, kw - 128:kw], psA[:, kw - 128:kw],
                                   sb["maskc"][:])
                v1 = k.inc(ins, "dve")
                k.wait(v, "dve", v1)
                ins = v.tensor_reduce(cols[:, 0:1], psA[:, 0:kw], axis=AX.X,
                                      op=ALU.max, negate=True)
                m[("nm", i, h)] = k.inc(ins, "dve")
            k.sec("vector", f_sm)

            def f_exp(sc, i=i, h=h, kw=kw):
                k.wait(sc, "dve", m[("nm", i, h)])
                ins = sc.activation(cols[:, 1:2], cols[:, 0:1], AF.Copy, scale=CS)
                v1 = k.inc(ins, "act")
                k.wait(sc, "act", v1)
                if h >= 2:
                    k.wait(sc, "pe", m[("trd", i, h - 2)])
                ins = sc.activation(pb[h % 2][:, 0:kw], psA[:, 0:kw], AF.Exp,
                                    bias=cols[:, 1:2], scale=CS,
                                    accum_out=cols[:, 2:3])
                m[("exp", i, h)] = k.inc(ins, "act")
            k.sec("scalar", f_exp)

            def f_rz(v, i=i, h=h):
                k.wait(v, "act", m[("exp", i, h)])
                ins = v.reciprocal(cols[:, 3:4], cols[:, 2:3])
                m[("rzf", i, h)] = k.inc(ins, "dve")
            k.sec("vector", f_rz)

            jobs = []
            for j in range(nk):
                jobs.append((pb[h % 2][:, j * 128:(j + 1) * 128],
                             pt[h % 2][:, j * 128:(j + 1) * 128],
                             [("act", m[("exp", i, h)])]))
            transpose_block(k, sb, jobs, slots, "fat")
            m[("trd", i, h)] = k.cnt["pe"]
            m[("tev", i, h)] = (k.cnt["dve"], k.cnt["act"])

            def f_yh(te, i=i, h=h, nk=nk):
                k.wait(te, "dve", m[("tev", i, h)][0])
                k.wait(te, "act", m[("tev", i, h)][1])
                k.guard_wait(te, ("psY",))
                ins = None
                for j in range(nk):
                    ins = te.matmul(psY[:, 0:128], pt[h % 2][:, j * 128:(j + 1) * 128],
                                    v_sh[:, j * 128:(j + 1) * 128],
                                    start=(j == 0), stop=(j == nk - 1))
                m[("yh", i, h)] = k.inc(ins, "pe")
            k.sec("tensor", f_yh)

            def f_fe(v, i=i, h=h):
                k.wait(v, "pe", m[("yh", i, h)])
                k.wait(v, "dve", m[("rzf", i, h)])
                ins = v.tensor_scalar(Ft[:, h * 128:(h + 1) * 128], psY[:, 0:128],
                                      cols[:, 3:4], None, ALU.mult)
                m[("fe", i, h)] = k.inc(ins, "dve")
                k.guard_set(("psY",), "dve")
            k.sec("vector", f_fe)

        def f_add(v, i=i):
            k.wait(v, "sdl", m[("x", i)])
            k.wait(v, "dve", m[("fe", i, 15)])
            ins = v.tensor_add(xr[i % 2][:], xr[i % 2][:], Ft[:])
            m[("add", i)] = k.inc(ins, "dve")
        k.sec("vector", f_add)

        def f_st(s, i=i):
            k.wait(s, "dve", m[("add", i)])
            ins = s.dma_start(io["hbm_x1"][i * 128:(i + 1) * 128, :], xr[i % 2][:])
            m[("st", i)] = k.inc(ins, "sds")
        k.sec("sync", f_st)



def ph_ln1(k, io, sb, pstack, h0T, xn1T):
    k.prologue()
    xf = [k.sb([128, 2048], F32, OFF_BIG2 + i * 8192, f"ln1xf{i}") for i in range(2)]
    xc = [k.sb([128, 2048], F32, OFF_BIG2 + 16384 + i * 8192, f"ln1xc{i}") for i in range(2)]
    xq = k.sb([128, 2048], F32, OFF_BIG2 + 32768, "ln1xq")
    ub = [k.sb([128, 2048], BF16, OFF_BIG2 + 40960 + i * 4096, f"ln1ub{i}") for i in range(2)]
    xnb = [k.sb([128, 2048], BF16, OFF_BIG2 + 49152 + i * 4096, f"ln1xb{i}") for i in range(2)]
    cols = k.sb([128, 8 * ST], F32, OFF_RS, "ln1cols")
    nw = k.sb([128, 16], F32, OFF_RS + 10240, "nwsb")
    pts = [pstack.enter_context(k.nc.psum_tensor(f"ln1pt{i}", [128, 1024], BF16))
           for i in range(8)]
    slots = [p[:, 0:128] for p in pts]
    m = {}

    def f_nw(s):
        m["nw"] = k.inc(s.dma_start(nw[:], io["nw2d"]), "sdl")
    k.sec("sync", f_nw)

    for t in range(ST):
        b = t % 2

        def f_load(s, t=t, b=b):
            if t >= 2:
                k.wait(s, "dve", m[("x2", t - 2)])
            m[("ld", t)] = k.inc(
                s.dma_start(xf[b][:], io["hbm_x1"][t * 128:(t + 1) * 128, :]), "sdl")
        k.sec("sync", f_load)

        def f_sum(v, t=t, b=b):
            k.wait(v, "sdl", m[("ld", t)])
            ins = v.tensor_reduce(cols[:, t:t + 1], xf[b][:], axis=AX.X, op=ALU.add)
            m[("sum", t)] = k.inc(ins, "dve")
        k.sec("vector", f_sum)

        def f_nm(sc, t=t):
            k.wait(sc, "dve", m[("sum", t)])
            ins = sc.activation(cols[:, ST + t:ST + t + 1], cols[:, t:t + 1], AF.Copy,
                                scale=-1.0 / D)
            m[("nm", t)] = k.inc(ins, "act")
        k.sec("scalar", f_nm)

        def f_xc(v, t=t, b=b):
            k.wait(v, "act", m[("nm", t)])
            if t >= 2:
                k.wait(v, "act", m[("sq2", t - 2)])
            ins = v.tensor_scalar_add(xc[b][:], xf[b][:], cols[:, ST + t:ST + t + 1])
            m[("xc", t)] = k.inc(ins, "dve")
        k.sec("vector", f_xc)

        def f_sq(sc, t=t, b=b):
            k.wait(sc, "dve", m[("xc", t)])
            ins = sc.activation(xq[:], xc[b][:], AF.Square,
                                accum_out=cols[:, 2 * ST + t:2 * ST + t + 1])
            v1 = k.inc(ins, "act")
            k.wait(sc, "act", v1)
            ins = sc.activation(cols[:, 3 * ST + t:3 * ST + t + 1],
                                cols[:, 2 * ST + t:2 * ST + t + 1], AF.Sqrt,
                                scale=1.0 / D, bias=sb["misc"][:, 0:1])
            m[("sq", t)] = k.inc(ins, "act")
        k.sec("scalar", f_sq)

        def f_xn(v, t=t, b=b):
            k.wait(v, "act", m[("sq", t)])
            ins = v.reciprocal(cols[:, 4 * ST + t:4 * ST + t + 1],
                               cols[:, 3 * ST + t:3 * ST + t + 1])
            v1 = k.inc(ins, "dve")
            k.wait(v, "dve", v1)
            ins = v.tensor_scalar(xc[b][:], xc[b][:], cols[:, 4 * ST + t:4 * ST + t + 1],
                                  None, ALU.mult)
            m[("xn", t)] = k.inc(ins, "dve")
        k.sec("vector", f_xn)

        def f_sq2(sc, t=t, b=b):
            k.wait(sc, "dve", m[("xn", t)])
            ins = sc.activation(xq[:], xc[b][:], AF.Square,
                                accum_out=cols[:, 5 * ST + t:5 * ST + t + 1])
            v1 = k.inc(ins, "act")
            k.wait(sc, "act", v1)
            ins = sc.activation(cols[:, 6 * ST + t:6 * ST + t + 1],
                                cols[:, 5 * ST + t:5 * ST + t + 1], AF.Sqrt,
                                scale=1.0 / D, bias=sb["misc"][:, 1:2])
            m[("sq2", t)] = k.inc(ins, "act")
        k.sec("scalar", f_sq2)

        def f_ub(v, t=t, b=b):
            k.wait(v, "act", m[("sq2", t)])
            ins = v.reciprocal(cols[:, 7 * ST + t:7 * ST + t + 1],
                               cols[:, 6 * ST + t:6 * ST + t + 1])
            v1 = k.inc(ins, "dve")
            k.wait(v, "dve", v1)
            if t >= 2:
                k.wait(v, "pe", m[("trd", t - 2)])
            ins = v.tensor_scalar(ub[b][:], xc[b][:], cols[:, 7 * ST + t:7 * ST + t + 1],
                                  None, ALU.mult)
            m[("ub", t)] = k.inc(ins, "dve")
        k.sec("vector", f_ub)

        def f_xnb(g, t=t, b=b):
            k.wait(g, "dve", m[("xn", t)])
            if t >= 2:
                k.wait(g, "pe", m[("trd", t - 2)])
            ins = g.tensor_copy(xnb[b][:], xc[b][:])
            m[("xnb", t)] = k.inc(ins, "pool")
        k.sec("gpsimd", f_xnb)

        def f_x2(v, t=t, b=b):
            k.wait(v, "pool", m[("xnb", t)])
            k.wait(v, "dve", m[("ub", t)])
            ins = v.tensor_add(xf[b][:], xf[b][:], xc[b][:])
            m[("x2", t)] = k.inc(ins, "dve")
        k.sec("vector", f_x2)

        def f_st(s, t=t, b=b):
            k.wait(s, "dve", m[("x2", t)])
            ins = s.dma_start(io["hbm_out"][t * 128:(t + 1) * 128, :], xf[b][:])
            m[("stx2", t)] = k.inc(ins, "sds")
        k.sec("sync", f_st)

        jobs = []
        for j in range(DT):
            jobs.append((ub[b][:, j * 128:(j + 1) * 128],
                         h0T[:, j * 2048 + t * 128: j * 2048 + (t + 1) * 128],
                         [("dve", m[("ub", t)]), ("sdl", m["nw"])],
                         nw[:, j:j + 1]))
            jobs.append((xnb[b][:, j * 128:(j + 1) * 128],
                         xn1T[:, j * 2048 + t * 128: j * 2048 + (t + 1) * 128],
                         [("pool", m[("xnb", t)])]))
        transpose_block(k, sb, jobs, slots, "ln1pt")
        m[("trd", t)] = k.cnt["pe"]

    end = k.snap()

    def f_spill(s):
        k.wait_snap(s, end)
        k.inc(s.dma_start(io["hbm_xn1T"], xn1T[:]), "sds")
    k.sec("sync", f_spill)



def emit_lora_tmp(k, io, sb, pstack, at_name, at_sb, tmp_sb, inT, m, tag):
    def f_w(g):
        src = io[at_name].rearrange("(dt p) c -> p dt c", p=128)
        dst = at_sb[:].rearrange("p (dt c) -> p dt c", dt=DT)
        m[(tag, "w")] = k.inc(g.dma_start(dst, src), "sdw")
    k.sec("gpsimd", f_w)

    psL = pstack.enter_context(k.nc.psum_tensor(f"{tag}ps", [16, 1024], F32))
    for ch in range(4):
        def f_mm(te, ch=ch):
            k.wait(te, "sdw", m[(tag, "w")])
            k.guard_wait(te, (tag, ch % 2))
            ps = psL[:, (ch % 2) * 512:(ch % 2) * 512 + 512]
            pairs = [(at_sb[:, d * 16:(d + 1) * 16],
                      inT[:, d * 2048 + ch * 512: d * 2048 + (ch + 1) * 512])
                     for d in range(DT)]
            m[(tag, "mm", ch)] = mm_group(k, te, ps, pairs)
        k.sec("tensor", f_mm)

        def f_ev(sc, ch=ch):
            k.wait(sc, "pe", m[(tag, "mm", ch)])
            ps = psL[:, (ch % 2) * 512:(ch % 2) * 512 + 512]
            ins = sc.activation(tmp_sb[:, ch * 512:(ch + 1) * 512], ps, AF.Copy,
                                scale=1.0 / 16.0)
            m[(tag, "ev", ch)] = k.inc(ins, "act")
            k.guard_set((tag, ch % 2), "act")
        k.sec("scalar", f_ev)



def ph_l1v(k, io, sb, pstack, h0T, outbuf):
    k.prologue()
    m = {}
    vat = k.sb([128, 256], BF16, OFF_RS + 9728, "vatsb")
    tmpv = k.sb([16, 2048], BF16, OFF_BIG1 + 36864, "tmpvsb")
    vbt = k.sb([16, 2048], BF16, OFF_BIG1 + 40960, "vbtsb")
    wch = [k.sb([128, 16 * 512], BF16, OFF_BIG1 + i * 16384, f"l1vc{i}") for i in range(2)]
    psA = pstack.enter_context(k.nc.psum_tensor("l1vps", [128, 2048], F32))

    def f_vbt(g):
        m["vbt"] = k.inc(g.dma_start(vbt[:], io["vBT"]), "sdw")
    k.sec("gpsimd", f_vbt)

    emit_lora_tmp(k, io, sb, pstack, "vAT", vat, tmpv, h0T, m, "lv")

    for ech in range(4):
        def f_w(g, ech=ech):
            if ech >= 2:
                k.wait(g, "pe", m[("mmd", ech - 2)])
            src = io["l1wvT"][:, ech * 512:(ech + 1) * 512].rearrange(
                "(dt p) c -> p dt c", p=128)
            dst = wch[ech % 2][:].rearrange("p (dt c) -> p dt c", dt=DT)
            m[("w", ech)] = k.inc(g.dma_start(dst, src), "sdw")
        k.sec("gpsimd", f_w)

        for st in range(ST):
            def f_mm(te, ech=ech, st=st):
                k.wait(te, "sdw", m[("w", ech)])
                if ech == 0 and st == 0:
                    k.wait(te, "sdw", m["vbt"])
                    k.wait(te, "act", m[("lv", "ev", 3)])
                k.guard_wait(te, ("psA", st % 4))
                ps = psA[:, (st % 4) * 512:(st % 4 + 1) * 512]
                pairs = [(h0T[:, d * 2048 + st * 128: d * 2048 + (st + 1) * 128],
                          wch[ech % 2][:, d * 512:(d + 1) * 512]) for d in range(DT)]
                pairs.append((tmpv[:, st * 128:(st + 1) * 128],
                              vbt[:, ech * 512:(ech + 1) * 512]))
                m[("mm", ech, st)] = mm_group(k, te, ps, pairs)
                if st == ST - 1:
                    m[("mmd", ech)] = m[("mm", ech, st)]
            k.sec("tensor", f_mm)

            engname = "dve" if st % 2 == 0 else "act"

            def f_ev(e, ech=ech, st=st, engname=engname):
                ps = psA[:, (st % 4) * 512:(st % 4 + 1) * 512]
                evac(k, e, engname,
                     outbuf[:, st * 2048 + ech * 512: st * 2048 + (ech + 1) * 512],
                     ps, wait=("pe", m[("mm", ech, st)]))
                k.guard_set(("psA", st % 4), engname)
            k.sec("vector" if engname == "dve" else "scalar", f_ev)



def ph_l1at(k, io, sb, pstack, h0T, vbuf):
    k.prologue()
    CS = 0.08838834764831845
    m = {}
    B1 = OFF_BIG1
    cosb = k.sb([128, 2048], F32, B1, "cossb")
    sinb = k.sb([128, 2048], F32, B1 + 8192, "sinsb")
    wqs = [k.sb([128, 2048], BF16, B1 + 16384 + i * 4096, f"wqs{i}") for i in range(2)]
    wks = [k.sb([128, 2048], BF16, B1 + 24576 + i * 4096, f"wks{i}") for i in range(2)]
    Qh = k.sb([128, 2048], BF16, B1 + 32768, "Qhsb")
    Kh = k.sb([128, 2048], BF16, B1 + 36864, "Khsb")
    Qr = k.sb([128, 2048], BF16, B1 + 40960, "Qrsb")
    Kr = k.sb([128, 2048], BF16, B1 + 45056, "Krsb")
    rs1 = k.sb([128, 2048], BF16, B1 + 49152, "rs1sb")
    rs2 = k.sb([128, 2048], BF16, B1 + 53248, "rs2sb")
    pb = k.sb([128, 640], BF16, B1 + 57344, "l1pb")
    pt = k.sb([128, 640], BF16, B1 + 58880, "l1pt")
    yT = k.sb([128, 2048], BF16, B1 + 60416, "l1yT")
    tmpq = k.sb([16, 2048], BF16, OFF_RS + 256, "tmpqsb")
    tmpk = k.sb([16, 2048], BF16, OFF_RS + 4352, "tmpksb")
    qat = k.sb([128, 256], BF16, OFF_RS + 9728, "qatsb")
    kat = k.sb([128, 256], BF16, OFF_RS + 10240, "katsb")
    qbth = [k.sb([16, 128], BF16, OFF_RS + 8448 + i * 256, f"qbth{i}") for i in range(2)]
    kbth = [k.sb([16, 128], BF16, OFF_RS + 8960 + i * 256, f"kbth{i}") for i in range(2)]
    cols = k.sb([128, 4], F32, OFF_RS + 9472, "l1cols")
    psA = pstack.enter_context(k.nc.psum_tensor("l1aps", [128, 2048], F32))

    def f_cs(s):
        k.inc(s.dma_start(cosb[:], io["c_cosT"]), "sdl")
        m["cs"] = k.inc(s.dma_start(sinb[:], io["c_sinT"]), "sdl")
    k.sec("sync", f_cs)

    with ExitStack() as lstack:
        emit_lora_tmp(k, io, sb, lstack, "qAT", qat, tmpq, h0T, m, "lq")
        emit_lora_tmp(k, io, sb, lstack, "kAT", kat, tmpk, h0T, m, "lk")
    psY = pstack.enter_context(k.nc.psum_tensor("l1ay", [128, 512], F32))
    pts = [pstack.enter_context(k.nc.psum_tensor(f"l1at{i}", [128, 1024], BF16))
           for i in range(3)]
    slots = [p[:, 0:128] for p in pts]

    for h in range(16):
        hb = h % 2

        def f_wl(g, h=h, hb=hb):
            if h >= 2:
                k.wait(g, "pe", m[("qkd", h - 2)])
            src = io["l1wqT"][:, h * 128:(h + 1) * 128].rearrange(
                "(dt p) c -> p dt c", p=128)
            dst = wqs[hb][:].rearrange("p (dt c) -> p dt c", dt=DT)
            k.inc(g.dma_start(dst, src), "sdw")
            src = io["l1wkT"][:, h * 128:(h + 1) * 128].rearrange(
                "(dt p) c -> p dt c", p=128)
            dst = wks[hb][:].rearrange("p (dt c) -> p dt c", dt=DT)
            k.inc(g.dma_start(dst, src), "sdw")
            src = io["qBT"][:, h * 128:(h + 1) * 128]
            k.inc(g.dma_start(qbth[hb][:], src), "sdw")
            src = io["kBT"][:, h * 128:(h + 1) * 128]
            m[("wl", h)] = k.inc(g.dma_start(kbth[hb][:], src), "sdw")
        k.sec("gpsimd", f_wl)

        for which, wsl, bt, tmpT, dst in (("q", wqs, "qbth", tmpq, Qh),
                                          ("kk", wks, "kbth", tmpk, Kh)):
            for ch in range(4):
                def f_mm(te, h=h, hb=hb, which=which, wsl=wsl, bt=bt, tmpT=tmpT, ch=ch):
                    k.wait(te, "sdw", m[("wl", h)])
                    if h == 0 and which == "q" and ch == 0:
                        k.wait(te, "act", m[("lq", "ev", 3)])
                        k.wait(te, "act", m[("lk", "ev", 3)])
                    if h >= 1 and which == "q" and ch == 0:
                        k.wait(te, "act", m[("exp", h - 1, ST - 1)])
                    k.guard_wait(te, ("qk", which, ch))
                    ps = psA[:, ch * 512:(ch + 1) * 512]
                    bts = {"qbth": qbth, "kbth": kbth}[bt][hb]
                    pairs = [(wsl[hb][:, d * 128:(d + 1) * 128],
                              h0T[:, d * 2048 + ch * 512: d * 2048 + (ch + 1) * 512])
                             for d in range(DT)]
                    pairs.append((bts[:], tmpT[:, ch * 512:(ch + 1) * 512]))
                    m[(which, h, ch)] = mm_group(k, te, ps, pairs)
                k.sec("tensor", f_mm)

                engname = "dve" if ch % 2 == 0 else "act"

                def f_ev(e, h=h, which=which, ch=ch, engname=engname, dst=dst):
                    if h >= 1:
                        k.wait(e, "dve" if engname == "dve" else "dve",
                               m.get(("roped", which, h - 1), 0))
                    ps = psA[:, ch * 512:(ch + 1) * 512]
                    evac(k, e, engname, dst[:, ch * 512:(ch + 1) * 512], ps,
                         wait=("pe", m[(which, h, ch)]))
                    k.guard_set(("qk", which, ch), engname)
                    m[(which + "ev", h, ch)] = k.cnt[engname]
                k.sec("vector" if engname == "dve" else "scalar", f_ev)
        m[("qkd", h)] = k.cnt["pe"]

        for which, src_, dst_, rsb in (("q", Qh, Qr, rs1), ("kk", Kh, Kr, rs2)):
            def f_rot_mm(te, h=h, which=which, src_=src_):
                k.wait(te, "act", m[(which + "ev", h, 3)])
                k.wait(te, "dve", m[(which + "ev", h, 2)])
                for ch in range(4):
                    k.guard_wait(te, ("qk", "q", ch))
                    k.guard_wait(te, ("qk", "kk", ch))
                    ps = psA[:, ch * 512:(ch + 1) * 512]
                    ins = te.matmul(ps, sb["rotT"][:],
                                    src_[:, ch * 512:(ch + 1) * 512],
                                    start=True, stop=True)
                m[("rot", which, h)] = k.inc(ins, "pe")
            k.sec("tensor", f_rot_mm)

            def f_rot_ev(v, h=h, which=which, rsb=rsb):
                k.wait(v, "pe", m[("rot", which, h)])
                ins = v.tensor_copy(rsb[:], psA[:, 0:2048])
                v1 = k.inc(ins, "dve")
                k.wait(v, "dve", v1)
                k.wait(v, "sdl", m["cs"])
                ins = v.tensor_mul(rsb[:], rsb[:], sinb[:])
                m[("rsin", which, h)] = k.inc(ins, "dve")
                for ch in range(4):
                    k.guard_set(("qk", which, ch), "dve")
            k.sec("vector", f_rot_ev)

            def f_cos(g, h=h, which=which, src_=src_, dst_=dst_):
                k.wait(g, "act", m[(which + "ev", h, 3)])
                k.wait(g, "dve", m[(which + "ev", h, 2)])
                k.wait(g, "sdl", m["cs"])
                if h >= 1:
                    k.wait(g, "pe", m[("attd", h - 1)])
                ins = g.tensor_mul(dst_[:], src_[:], cosb[:])
                m[("cosd", which, h)] = k.inc(ins, "pool")
            k.sec("gpsimd", f_cos)

            def f_radd(v, h=h, which=which, dst_=dst_, rsb=rsb):
                k.wait(v, "pool", m[("cosd", which, h)])
                k.wait(v, "dve", m[("rsin", which, h)])
                ins = v.tensor_add(dst_[:], dst_[:], rsb[:])
                m[("roped", which, h)] = k.inc(ins, "dve")
            k.sec("vector", f_radd)

        for i in range(ST):
            k0 = max(0, i - 4)
            nk = i + 1 - k0
            kw = nk * 128

            def f_mm(te, h=h, i=i, k0=k0, kw=kw):
                if i == 0:
                    k.wait(te, "dve", m[("roped", "q", h)])
                    k.wait(te, "dve", m[("roped", "kk", h)])
                else:
                    k.wait(te, "act", m[("exp", h, i - 1)])
                ins = None
                for c0 in range(0, kw, 512):
                    w = min(512, kw - c0)
                    ps = psA[:, c0:c0 + w]
                    ins = te.matmul(ps, Qr[:, i * 128:(i + 1) * 128],
                                    Kr[:, k0 * 128 + c0: k0 * 128 + c0 + w],
                                    start=True, stop=True)
                m[("mm", h, i)] = k.inc(ins, "pe")
            k.sec("tensor", f_mm)

            def f_msk(v, h=h, i=i, k0=k0, kw=kw, nk=nk):
                k.wait(v, "pe", m[("mm", h, i)])
                dcol = (i - k0) * 128
                ins = v.tensor_add(psA[:, dcol:dcol + 128], psA[:, dcol:dcol + 128],
                                   sb["maskc"][:])
                v1 = k.inc(ins, "dve")
                if i >= 4:
                    k.wait(v, "dve", v1)
                    ins = v.tensor_add(psA[:, 0:128], psA[:, 0:128], sb["masku"][:])
                    v1 = k.inc(ins, "dve")
                k.wait(v, "dve", v1)
                ins = v.tensor_reduce(cols[:, 0:1], psA[:, 0:kw], axis=AX.X,
                                      op=ALU.max, negate=True)
                m[("nmx", h, i)] = k.inc(ins, "dve")
            k.sec("vector", f_msk)

            def f_exp(sc, h=h, i=i, kw=kw):
                k.wait(sc, "dve", m[("nmx", h, i)])
                ins = sc.activation(cols[:, 1:2], cols[:, 0:1], AF.Copy, scale=CS)
                v1 = k.inc(ins, "act")
                k.wait(sc, "act", v1)
                if i >= 1:
                    k.wait(sc, "pe", m[("trp", h, i - 1)])
                ins = sc.activation(pb[:, 0:kw], psA[:, 0:kw], AF.Exp,
                                    bias=cols[:, 1:2], scale=CS,
                                    accum_out=cols[:, 2:3])
                m[("exp", h, i)] = k.inc(ins, "act")
            k.sec("scalar", f_exp)

            def f_div(v, h=h, i=i, kw=kw):
                k.wait(v, "act", m[("exp", h, i)])
                ins = v.reciprocal(cols[:, 3:4], cols[:, 2:3])
                v1 = k.inc(ins, "dve")
                k.wait(v, "dve", v1)
                ins = v.tensor_scalar(pb[:, 0:kw], pb[:, 0:kw], cols[:, 3:4],
                                      None, ALU.mult)
                m[("pdiv", h, i)] = k.inc(ins, "dve")
            k.sec("vector", f_div)

            jobs = []
            for j in range(nk):
                jobs.append((pb[:, j * 128:(j + 1) * 128],
                             pt[:, j * 128:(j + 1) * 128],
                             [("dve", m[("pdiv", h, i)])]))
            transpose_block(k, sb, jobs, slots, "l1tr")
            m[("tev", h, i)] = (k.cnt["dve"], k.cnt["act"])
            m[("trp", h, i)] = k.cnt["pe"]

            def f_yh(te, h=h, i=i, k0=k0, nk=nk):
                k.wait(te, "dve", m[("tev", h, i)][0])
                k.wait(te, "act", m[("tev", h, i)][1])
                k.guard_wait(te, ("psY",))
                ins = None
                for j in range(nk):
                    ins = te.matmul(psY[:, 0:128],
                                    vbuf[:, (k0 + j) * 2048 + h * 128: (k0 + j) * 2048 + (h + 1) * 128],
                                    pt[:, j * 128:(j + 1) * 128],
                                    start=(j == 0), stop=(j == nk - 1))
                m[("yh", h, i)] = k.inc(ins, "pe")
            k.sec("tensor", f_yh)

            def f_ye(v, h=h, i=i):
                if i == 0 and h >= 2:
                    k.wait(v, "sds", m[("sp", h - 2)])
                evac(k, v, "dve", yT[:, i * 128:(i + 1) * 128], psY[:, 0:128],
                     wait=("pe", m[("yh", h, i)]))
                m[("ye", h, i)] = k.cnt["dve"]
                k.guard_set(("psY",), "dve")
            k.sec("vector", f_ye)
        m[("attd", h)] = k.cnt["pe"]

        def f_sp(s, h=h):
            k.wait(s, "dve", m[("ye", h, ST - 1)])
            ins = s.dma_start(io["hbm_yT"][:, h * 2048:(h + 1) * 2048], yT[:])
            m[("sp", h)] = k.inc(ins, "sds")
        k.sec("sync", f_sp)



def ph_l1o(k, io, sb, pstack, ybuf, wobuf):
    k.prologue()
    m = {}
    oat = k.sb([128, 256], BF16, OFF_RS + 9728, "oatsb")
    obt = k.sb([16, 2048], BF16, OFF_RS + 256, "obtsb")
    tmpo = k.sb([16, 2048], BF16, OFF_RS + 4352, "tmposb")
    xr = [k.sb([128, 2048], F32, OFF_BIG2 + i * 8192, f"l1oxr{i}") for i in range(2)]
    psA = pstack.enter_context(k.nc.psum_tensor("l1ops", [128, 2048], F32))

    def f_rl(s):
        m["y"] = k.inc(s.dma_start(ybuf[:], io["hbm_yT"]), "sdl")
    k.sec("sync", f_rl)

    def f_w(g):
        src = io["l1woT"].rearrange("(dt p) c -> p dt c", p=128)
        dst = wobuf[:].rearrange("p (dt c) -> p dt c", dt=DT)
        k.inc(g.dma_start(dst, src), "sdw")
        m["obt"] = k.inc(g.dma_start(obt[:], io["oBT"]), "sdw")
    k.sec("gpsimd", f_w)

    def f_wo2(g):
        src = io["oAT"].rearrange("(dt p) c -> p dt c", p=128)
        dst = oat[:].rearrange("p (dt c) -> p dt c", dt=DT)
        m["oat"] = k.inc(g.dma_start(dst, src), "sdw")
    k.sec("gpsimd", f_wo2)

    psL = pstack.enter_context(k.nc.psum_tensor("l1olps", [16, 1024], F32))
    for ch in range(4):
        def f_mm(te, ch=ch):
            k.wait(te, "sdw", m["oat"])
            k.wait(te, "sdl", m["y"])
            k.guard_wait(te, ("lo", ch % 2))
            ps = psL[:, (ch % 2) * 512:(ch % 2) * 512 + 512]
            pairs = [(oat[:, d * 16:(d + 1) * 16],
                      ybuf[:, d * 2048 + ch * 512: d * 2048 + (ch + 1) * 512])
                     for d in range(DT)]
            m[("lomm", ch)] = mm_group(k, te, ps, pairs)
        k.sec("tensor", f_mm)

        def f_ev(sc, ch=ch):
            k.wait(sc, "pe", m[("lomm", ch)])
            ps = psL[:, (ch % 2) * 512:(ch % 2) * 512 + 512]
            ins = sc.activation(tmpo[:, ch * 512:(ch + 1) * 512], ps, AF.Copy,
                                scale=1.0 / 16.0)
            m[("loev", ch)] = k.inc(ins, "act")
            k.guard_set(("lo", ch % 2), "act")
        k.sec("scalar", f_ev)

    for st in range(ST):
        def f_x(s, st=st):
            if st >= 2:
                k.wait(s, "sds", m[("st", st - 2)])
            m[("x", st)] = k.inc(
                s.dma_start(xr[st % 2][:], io["hbm_out"][st * 128:(st + 1) * 128, :]),
                "sdl")
        k.sec("sync", f_x)

        def f_mm(te, st=st):
            if st == 0:
                k.wait(te, "sdw", m["obt"])
                k.wait(te, "act", m[("loev", 3)])
            for ch in range(4):
                k.guard_wait(te, ("psA", ch))
                ps = psA[:, ch * 512:(ch + 1) * 512]
                pairs = [(ybuf[:, e * 2048 + st * 128: e * 2048 + (st + 1) * 128],
                          wobuf[:, e * 2048 + ch * 512: e * 2048 + (ch + 1) * 512])
                         for e in range(DT)]
                pairs.append((tmpo[:, st * 128:(st + 1) * 128],
                              obt[:, ch * 512:(ch + 1) * 512]))
                m[("mm", st, ch)] = mm_group(k, te, ps, pairs)
        k.sec("tensor", f_mm)

        def f_add(v, st=st):
            b = st % 2
            k.wait(v, "sdl", m[("x", st)])
            for ch in range(4):
                k.wait(v, "pe", m[("mm", st, ch)])
                ps = psA[:, ch * 512:(ch + 1) * 512]
                ins = v.tensor_add(xr[b][:, ch * 512:(ch + 1) * 512], ps,
                                   xr[b][:, ch * 512:(ch + 1) * 512])
                m[("add", st, ch)] = k.inc(ins, "dve")
                k.guard_set(("psA", ch), "dve")
        k.sec("vector", f_add)

        def f_st(s, st=st):
            k.wait(s, "dve", m[("add", st, 3)])
            ins = s.dma_start(io["hbm_out"][st * 128:(st + 1) * 128, :], xr[st % 2][:])
            m[("st", st)] = k.inc(ins, "sds")
        k.sec("sync", f_st)



def ph_l1f1(k, io, sb, pstack, xn1T, tbuf, zbuf):
    k.prologue()
    m = {}
    su = k.sb([128, 16], F32, OFF_RS + 8448, "suugsb")
    sudp = k.sb([128, 64], F32, OFF_RS + 8512, "sudpsb")
    wup = k.sb([128, 64], F32, OFF_RS + 8768, "wupsb")
    wgt = k.sb([128, 64], F32, OFF_RS + 9024, "wgtsb")
    cup = k.sb([128, 64], F32, OFF_RS + 9280, "cupsb")
    cgt = k.sb([128, 64], F32, OFF_RS + 9536, "cgtsb")
    hq = k.sb([128, 128], BF16, OFF_RS + 13632, "hqsb")
    wdrow = k.sb([1, 2048], BF16, OFF_BIG2 + 61440, "wdrow")
    onesr = k.sb([1, 128], BF16, OFF_RS + 14400, "onesr")
    wdpb = k.sb([128, 2048], F32, OFF_RS + 256, "wdpbsb")
    psA = pstack.enter_context(k.nc.psum_tensor("f1ps", [128, 2048], F32))
    m = {}

    def f_l(s):
        k.inc(s.dma_start(xn1T[:], io["hbm_xn1T"]), "sdl")
        k.inc(s.dma_start(su[:], io["suug2d"]), "sdl")
        k.inc(s.dma_start(sudp[:], io["sudp2d"]), "sdl")
        k.inc(s.dma_start(wup[:], io["wscup2d"]), "sdl")
        k.inc(s.dma_start(wgt[:], io["wscgate2d"]), "sdl")
        k.inc(s.dma_start(hq[:], io["c_hq"]), "sdl")
        m["l"] = k.inc(s.dma_start(onesr[:], io["c_onesr"]), "sdl")
    k.sec("sync", f_l)

    def f_wd(g):
        m["wd"] = k.inc(g.dma_start(wdrow[:], io["wscdp_row"]), "sdw")
    k.sec("gpsimd", f_wd)

    def f_sc(v):
        k.wait(v, "sdl", m["l"])
        ins = v.tensor_scalar(cup[:], wup[:], 64.0, None, ALU.mult)
        k.inc(ins, "dve")
        ins = v.tensor_scalar(cgt[:], wgt[:], 64.0, None, ALU.mult)
        m["sc"] = k.inc(ins, "dve")
    k.sec("vector", f_sc)

    def f_bc(te):
        k.wait(te, "sdw", m["wd"])
        k.wait(te, "sdl", m["l"])
        for ch in range(4):
            ins = te.matmul(psA[:, ch * 512:(ch + 1) * 512], onesr[:],
                            wdrow[:, ch * 512:(ch + 1) * 512], start=True, stop=True)
        m["bc"] = k.inc(ins, "pe")
    k.sec("tensor", f_bc)

    def f_bce(sc):
        k.wait(sc, "pe", m["bc"])
        ins = sc.activation(wdpb[:], psA[:], AF.Copy, scale=64.0)
        m["bce"] = k.inc(ins, "act")
    k.sec("scalar", f_bce)

    for d in range(DT):
        def f_t(v, d=d):
            if d == 0:
                k.wait(v, "dve", m["sc"])
            ins = v.tensor_scalar(tbuf[:, d * 2048:(d + 1) * 2048],
                                  xn1T[:, d * 2048:(d + 1) * 2048],
                                  su[:, d:d + 1], None, ALU.mult)
            m[("t", d)] = k.inc(ins, "dve")
        k.sec("vector", f_t)

    for d in range(DT):
        def f_h(te, d=d):
            k.wait(te, "dve", m[("t", d)])
            if d == 0:
                k.wait(te, "act", m["bce"])
            for ch in range(4):
                k.guard_wait(te, ("f1", ch))
                ps = psA[:, ch * 512:(ch + 1) * 512]
                ins = te.matmul(ps, hq[:], tbuf[:, d * 2048 + ch * 512: d * 2048 + (ch + 1) * 512],
                                start=True, stop=True)
                m[("h", d, ch)] = k.inc(ins, "pe")
        k.sec("tensor", f_h)

        for ch in range(4):
            engname = "dve" if ch % 2 == 0 else "act"

            def f_ev(e, d=d, ch=ch, engname=engname):
                ps = psA[:, ch * 512:(ch + 1) * 512]
                dstp = zbuf[:, d * 2048 + ch * 512: d * 2048 + (ch + 1) * 512]
                k.wait(e, "pe", m[("h", d, ch)])
                if engname == "dve":
                    ins = e.tensor_scalar(dstp, ps, 1.0 / 45.254833995939045, None,
                                          ALU.mult)
                    k.inc(ins, "dve")
                else:
                    ins = e.activation(dstp, ps, AF.Copy, scale=1.0 / 45.254833995939045)
                    k.inc(ins, "act")
                k.guard_set(("f1", ch), engname)
            k.sec("vector" if engname == "dve" else "scalar", f_ev)

    bufs = [zbuf, tbuf]
    cur = 0
    for stg, stride in enumerate([1, 2, 4, 8]):
        srcb = bufs[cur]
        dstb = bufs[1 - cur]
        start = k.snap()

        def f_bf_v(v, srcb=srcb, dstb=dstb, stride=stride, start=start):
            k.wait_snap(v, start)
            cnt = 0
            for i in range(DT):
                if i & stride:
                    continue
                a = srcb[:, i * 2048:(i + 1) * 2048]
                bb = srcb[:, (i + stride) * 2048:(i + stride + 1) * 2048]
                ins = v.tensor_add(dstb[:, i * 2048:(i + 1) * 2048], a, bb)
                k.inc(ins, "dve")
        k.sec("vector", f_bf_v)

        def f_bf_p(g, srcb=srcb, dstb=dstb, stride=stride, start=start):
            k.wait_snap(g, start)
            for i in range(DT):
                if i & stride:
                    continue
                a = srcb[:, i * 2048:(i + 1) * 2048]
                bb = srcb[:, (i + stride) * 2048:(i + stride + 1) * 2048]
                ins = g.tensor_sub(dstb[:, (i + stride) * 2048:(i + stride + 1) * 2048],
                                   a, bb)
                k.inc(ins, "pool")
        k.sec("gpsimd", f_bf_p)
        cur = 1 - cur
    assert cur == 0



def ph_wdtr(k, io, sb, pstack):
    k.prologue()
    m = {}
    wt = k.sb([128, 65536], BF16, OFF_BIG0, "wdtrT")
    sudp = k.sb([128, 64], F32, OFF_RS + 8512, "wdtrsu")
    hq = k.sb([128, 128], BF16, OFF_RS + 13632, "wdtrhq")
    psH = pstack.enter_context(k.nc.psum_tensor("wdtrps", [128, 2048], F32))

    def f_c(s):
        k.inc(s.dma_start(sudp[:], io["sudp2d"]), "sdl")
        m["c"] = k.inc(s.dma_start(hq[:], io["c_hq"]), "sdl")
    k.sec("sync", f_c)

    for oh in range(2):
        def f_l(g, oh=oh):
            if oh == 1:
                k.wait(g, "sds", m[("std", 0)])
            for jt in range(64):
                ins = g.dma_start(wt[:, jt * 1024:(jt + 1) * 1024],
                                  io["l1wdT"][jt * 128:(jt + 1) * 128,
                                              oh * 1024:(oh + 1) * 1024])
                m[("ld", oh, jt)] = k.inc(ins, "sdw")
        k.sec("gpsimd", f_l)

        for jt in range(64):
            def f_mm(te, oh=oh, jt=jt):
                k.wait(te, "sdw", m[("ld", oh, 63)])
                if oh == 0 and jt == 0:
                    k.wait(te, "sdl", m["c"])
                k.guard_wait(te, ("wh", jt % 2))
                ps = psH[:, (jt % 2) * 1024:(jt % 2) * 1024 + 1024]
                for cc in range(2):
                    ins = te.matmul(ps[:, cc * 512:(cc + 1) * 512], hq[:],
                                    wt[:, jt * 1024 + cc * 512: jt * 1024 + (cc + 1) * 512],
                                    start=True, stop=True)
                m[("hm", oh, jt)] = k.inc(ins, "pe")
            k.sec("tensor", f_mm)

            engname = "dve" if jt % 2 == 0 else "act"

            def f_ev(e, oh=oh, jt=jt, engname=engname):
                k.wait(e, "pe", m[("hm", oh, jt)])
                ps = psH[:, (jt % 2) * 1024:(jt % 2) * 1024 + 1024]
                dst = wt[:, jt * 1024:(jt + 1) * 1024]
                if engname == "dve":
                    ins = e.tensor_scalar(dst, ps, 1.0 / 90.50966799187809, None,
                                          ALU.mult)
                    k.inc(ins, "dve")
                else:
                    ins = e.activation(dst, ps, AF.Copy, scale=1.0 / 90.50966799187809)
                    k.inc(ins, "act")
                k.guard_set(("wh", jt % 2), engname)
            k.sec("vector" if engname == "dve" else "scalar", f_ev)

        for stride in WDTR_STRIDES:
            s1 = k.snap()
            pairs = [a for a in range(64) if not (a & stride)]
            gp = pairs[:10]
            vp = pairs[10:]

            def f_bg(g, stride=stride, s1=s1, gp=gp):
                k.wait_snap(g, s1)
                for a in gp:
                    av = wt[:, a * 1024:(a + 1) * 1024]
                    bv = wt[:, (a + stride) * 1024:(a + stride + 1) * 1024]
                    k.inc(g.tensor_sub(bv, av, bv), "pool")
            k.sec("gpsimd", f_bg)
            pv = k.cnt["pool"]

            def f_bv(v, stride=stride, s1=s1, vp=vp, gp=gp, pv=pv):
                k.wait_snap(v, s1)
                for a in vp:
                    av = wt[:, a * 1024:(a + 1) * 1024]
                    bv = wt[:, (a + stride) * 1024:(a + stride + 1) * 1024]
                    k.inc(v.tensor_sub(bv, av, bv), "dve")
                    k.inc(v.scalar_tensor_tensor(av, av, 2.0, bv,
                                                 ALU.mult, ALU.subtract), "dve")
                k.wait(v, "pool", pv)
                for a in gp:
                    av = wt[:, a * 1024:(a + 1) * 1024]
                    bv = wt[:, (a + stride) * 1024:(a + stride + 1) * 1024]
                    k.inc(v.scalar_tensor_tensor(av, av, 2.0, bv,
                                                 ALU.mult, ALU.subtract), "dve")
            k.sec("vector", f_bv)

        pvf = k.cnt["pool"]
        avf = k.cnt["act"]

        def f_su(v, oh=oh, pvf=pvf, avf=avf):
            k.wait(v, "pool", pvf)
            k.wait(v, "act", avf)
            k.wait(v, "sdl", m["c"])
            for jt in range(64):
                ins = v.tensor_scalar(wt[:, jt * 1024:(jt + 1) * 1024],
                                      wt[:, jt * 1024:(jt + 1) * 1024],
                                      sudp[:, jt:jt + 1], None, ALU.mult)
                m[("su", oh, jt)] = k.inc(ins, "dve")
        k.sec("vector", f_su)

        def f_st(s, oh=oh):
            for jt in range(64):
                k.wait(s, "dve", m[("su", oh, jt)])
                for go in range(4):
                    gb = oh * 4 + go
                    dst = io["hbm_wd2"][:, gb * 16384 + jt * 256:
                                        gb * 16384 + (jt + 1) * 256]
                    ins = s.dma_start(dst, wt[:, jt * 1024 + go * 256:
                                              jt * 1024 + go * 256 + 256])
                    k.inc(ins, "sds")
            m[("std", oh)] = k.cnt["sds"]
        k.sec("sync", f_st)



def ph_l1f2(k, io, sb, pstack, zbuf, w2buf, sp, first, prev_m):
    if first:
        k.prologue()
    s0 = k.snap()
    m = {}
    B0 = OFF_BIG0
    ups = [k.sb([128, 2048], BF16, B0 + i * 4096, f"f2up{i}_{sp}") for i in range(2)]
    gts = [k.sb([128, 2048], BF16, B0 + 8192 + i * 4096, f"f2gt{i}_{sp}") for i in range(2)]
    sil = [k.sb([128, 512], F32, B0 + 16384 + i * 2048, f"f2sil{i}_{sp}") for i in range(2)]
    upv = [k.sb([128, 512], F32, B0 + 20480 + i * 2048, f"f2upv{i}_{sp}") for i in range(2)]
    tmp = [k.sb([128, 256], F32, OFF_RS + 10240 + i * 1024, f"f2tm{i}_{sp}") for i in range(2)]
    wdb = [k.sb([128, 4096], BF16, B0 + 24576 + i * 8192, f"f2wd{i}_{sp}") for i in range(5)]
    cup = k.sb([128, 64], F32, OFF_RS + 9280, f"cupsb2{sp}")
    cgt = k.sb([128, 64], F32, OFF_RS + 9536, f"cgtsb2{sp}")
    wdpb = k.sb([128, 2048], F32, OFF_RS + 256, f"wdpbsb2{sp}")
    psA = pstack.enter_context(k.nc.psum_tensor(f"f2ps{sp}", [128, 2048], F32))
    psD = pstack.enter_context(k.nc.psum_tensor(f"f2pd{sp}", [128, 1024], F32))

    def emit_wd_load(c):
        gb, q = c // 4, c % 4

        def f_wd(s, c=c, gb=gb, q=q):
            if c < 5:
                if not first:
                    k.wait(s, "pe", s0["pe"])
            else:
                k.wait(s, "pe", m[("dmmd", (c - 5) // 4)])
            src = io["hbm_wd2"][:, gb * 16384 + q * 4096:
                                gb * 16384 + (q + 1) * 4096]
            sem = f"wd{c % 5}"
            m[("wdc", gb, q)] = (sem, k.inc(s.dma_start(wdb[c % 5][:], src), sem))
        k.sec("sync", f_wd)

    for c in range(5):
        emit_wd_load(c)

    for jt in range(64):
        jb = jt % 2

        def f_w(g, jt=jt, jb=jb):
            if jt < 2:
                if not first:
                    k.wait(g, "pe", prev_m[("mmd", 62 + jt)])
            else:
                k.wait(g, "pe", m[("mmd", jt - 2)])
            src = io["l1wupT"][:, jt * 128:(jt + 1) * 128].rearrange(
                "(dt p) c -> p dt c", p=128)
            dst = ups[jb][:].rearrange("p (dt c) -> p dt c", dt=DT)
            k.inc(g.dma_start(dst, src), "sdw")
            src = io["l1wgateT"][:, jt * 128:(jt + 1) * 128].rearrange(
                "(dt p) c -> p dt c", p=128)
            dst = gts[jb][:].rearrange("p (dt c) -> p dt c", dt=DT)
            m[("w", jt)] = k.inc(g.dma_start(dst, src), "sdw")
        k.sec("gpsimd", f_w)

        def f_mm(te, jt=jt, jb=jb):
            k.wait(te, "sdw", m[("w", jt)])
            k.guard_wait(te, ("f2u", jt % 2))
            psU = psA[:, (jt % 2) * 512:(jt % 2) * 512 + 512]
            pairs = [(ups[jb][:, d * 128:(d + 1) * 128],
                      zbuf[:, d * 2048 + sp * 512: d * 2048 + (sp + 1) * 512])
                     for d in range(DT)]
            m[("mmu", jt)] = mm_group(k, te, psU, pairs)
            k.guard_wait(te, ("f2g", jt % 2))
            psG = psA[:, 1024 + (jt % 2) * 512: 1024 + (jt % 2) * 512 + 512]
            pairs = [(gts[jb][:, d * 128:(d + 1) * 128],
                      zbuf[:, d * 2048 + sp * 512: d * 2048 + (sp + 1) * 512])
                     for d in range(DT)]
            m[("mmg", jt)] = mm_group(k, te, psG, pairs)
            m[("mmd", jt)] = m[("mmg", jt)]
        k.sec("tensor", f_mm)

        def f_sil(sc, jt=jt, jb=jb):
            k.wait(sc, "pe", m[("mmg", jt)])
            if jt < 2:
                if not first:
                    k.wait(sc, "dve", prev_m[("t2", 62 + jt)])
            else:
                k.wait(sc, "dve", m[("t2", jt - 2)])
            psG = psA[:, 1024 + (jt % 2) * 512: 1024 + (jt % 2) * 512 + 512]
            ins = sc.activation(sil[jb][:], psG, AF.Silu, scale=cgt[:, jt:jt + 1])
            m[("sil", jt)] = k.inc(ins, "act")
            k.guard_set(("f2g", jt % 2), "act")
        k.sec("scalar", f_sil)

        def f_upv(v, jt=jt, jb=jb):
            k.wait(v, "pe", m[("mmu", jt)])
            psU = psA[:, (jt % 2) * 512:(jt % 2) * 512 + 512]
            ins = v.tensor_scalar(upv[jb][:], psU, cup[:, jt:jt + 1],
                                  None, ALU.mult)
            m[("upv", jt)] = k.inc(ins, "dve")
            k.guard_set(("f2u", jt % 2), "dve")
        k.sec("vector", f_upv)

        def f_t2(v, jt=jt, jb=jb):
            k.wait(v, "act", m[("sil", jt)])
            if jt == 0 and not first:
                k.wait(v, "pe", s0["pe"])
            ins = v.tensor_mul(w2buf[:, jt * 512:(jt + 1) * 512], sil[jb][:],
                               upv[jb][:])
            m[("t2", jt)] = k.inc(ins, "dve")
        k.sec("vector", f_t2)

    for gb in range(8):
        for ls in range(4):
            def f_mm(te, gb=gb, ls=ls):
                if gb == 0 and ls == 0:
                    k.wait(te, "dve", m[("t2", 63)])
                k.guard_wait(te, ("psD", ls % 2))
                ps = psD[:, (ls % 2) * 512:(ls % 2) * 512 + 256]
                ins = None
                for j in range(64):
                    if j % 16 == 0:
                        sem, val = m[("wdc", gb, j // 16)]
                        k.wait(te, sem, val)
                    lhsT = w2buf[:, j * 512 + ls * 128: j * 512 + (ls + 1) * 128]
                    rhs = wdb[((gb * 4) + j // 16) % 5][:, (j % 16) * 256:
                                                        (j % 16 + 1) * 256]
                    ins = te.matmul(ps, lhsT, rhs, start=(j == 0), stop=(j == 63))
                m[("dmm", gb, ls)] = k.inc(ins, "pe")
                if ls == 3:
                    m[("dmmd", gb)] = m[("dmm", gb, ls)]
            k.sec("tensor", f_mm)

            def f_ev(v, gb=gb, ls=ls):
                k.wait(v, "pe", m[("dmm", gb, ls)])
                idx = gb * 4 + ls
                pp = idx % 2
                if idx < 2:
                    if not first:
                        k.wait(v, "sds", prev_m[("acc", 30 + idx)])
                else:
                    k.wait(v, "sds", m[("acc", idx - 2)])
                ps = psD[:, (ls % 2) * 512:(ls % 2) * 512 + 256]
                ins = v.tensor_mul(tmp[pp][:], ps, wdpb[:, gb * 256:(gb + 1) * 256])
                m[("ev", gb, ls)] = k.inc(ins, "dve")
                k.guard_set(("psD", ls % 2), "dve")
            k.sec("vector", f_ev)

            def f_acc(g, gb=gb, ls=ls):
                k.wait(g, "dve", m[("ev", gb, ls)])
                pp = (gb * 4 + ls) % 2
                rows = (sp * 4 + ls) * 128
                dst = io["hbm_out"][rows:rows + 128, gb * 256:(gb + 1) * 256]
                ins = g.dma_start(dst, tmp[pp][:], accum_op=ALU.add)
                m[("acc", gb * 4 + ls)] = k.inc(ins, "sds")
            k.sec("gpsimd", f_acc)

        for c in range(4 * gb + 5, min(4 * gb + 9, 32)):
            emit_wd_load(c)
    return m



IO_SPECS = {
    "x": ((S, D), F32),
    "l0wq": ((D, D), F32),
    "l0wk": ((D, D), F32),
    "l0wv": ((D, D), F32),
    "l0wo": ((D, D), F32),
    "hbm_xnT": ((128, DT * S), BF16),
    "hbm_q": ((128, ST * D), BF16),
    "hbm_v0T": ((128, DT * S), BF16),
    "hbm_x1": ((S, D), F32),
    "wgT": ((D, 16), F32),
    "wqkvT": ((D, 3 * D), F32),
    "l1wqT": ((D, D), F32),
    "l1wkT": ((D, D), F32),
    "l1wvT": ((D, D), F32),
    "l1woT": ((D, D), F32),
    "qAT": ((D, 16), F32),
    "kAT": ((D, 16), F32),
    "vAT": ((D, 16), F32),
    "oAT": ((D, 16), F32),
    "qBT": ((16, D), F32),
    "kBT": ((16, D), F32),
    "vBT": ((16, D), F32),
    "oBT": ((16, D), F32),
    "nw2d": ((128, 16), F32),
    "suug2d": ((128, 16), F32),
    "sudp2d": ((128, 64), F32),
    "wscup2d": ((128, 64), F32),
    "wscgate2d": ((128, 64), F32),
    "wscdp_row": ((1, D), F32),
    "l1wupT": ((D, 8192), F32),
    "l1wgateT": ((D, 8192), F32),
    "l1wdT": ((8192, D), F32),
    "hbm_xn1T": ((128, DT * S), BF16),
    "hbm_yT": ((128, DT * S), BF16),
    "hbm_wd2": ((128, 131072), BF16),
    "hbm_out": ((S, D), F32),
}


def build_group(phases, io_kinds):
    consts = _consts()
    nc = bass.Bass("TRN2", target_bir_lowering=False)
    io = {}
    for name, (shape, dtype) in IO_SPECS.items():
        kind = io_kinds.get(name, "Internal")
        io[name] = nc.dram_tensor(name, list(shape), dtype, kind=kind).ap()
    cst = {nm: nc.inline_tensor(consts[nm], name=f"c_{nm}").ap()
           for nm in ["ident", "hq", "mask_causal", "mask_upper", "cosT", "sinT",
                      "misc", "onesr", "rotT"]}
    io["c_cosT"] = cst["cosT"]
    io["c_sinT"] = cst["sinT"]
    io["c_hq"] = cst["hq"]
    io["c_onesr"] = cst["onesr"]
    io["c_rotT"] = cst["rotT"]

    stack = ExitStack()
    with stack:
        block = stack.enter_context(nc.Block())
        k = K(nc, stack, block)
        state = {}
        sb = {}
        sb["ident"] = k.sb([128, 128], BF16, OFF_RS + 12288, "identsb")
        sb["misc"] = k.sb([128, 4], F32, OFF_RS + 12544, "miscsb")
        sb["maskc"] = k.sb([128, 128], F32, OFF_RS + 12608, "maskcsb")
        sb["masku"] = k.sb([128, 128], F32, OFF_RS + 13120, "maskusb")
        sb["rotT"] = k.sb([128, 128], BF16, OFF_RS + 14656, "rotTsb")
        sb["ind"] = k.sb([128, 256], F32, OFF_RS + 256, "indsb")
        sb["rz"] = k.sb([128, 16], F32, OFF_RS + 1280, "rzsb")
        sb["k_sh"] = k.sb([128, 2048], BF16, OFF_RS + 1344, "kshsb")
        sb["k_shT"] = k.sb([128, 2048], BF16, OFF_RS + 5440, "kshTsb")
        sb["v_sh"] = k.sb([128, 2048], BF16, OFF_RS + 1344, "vshsb")
        sb["xnT"] = k.sb([128, 32768], BF16, OFF_BIG0, "xnTb")
        big1 = k.sb([128, 32768], BF16, OFF_BIG1, "big1")
        big2 = k.sb([128, 32768], BF16, OFF_BIG2, "big2")

        def f_c(s):
            k.inc(s.dma_start(sb["ident"][:], cst["ident"]), "sdl")
            k.inc(s.dma_start(sb["misc"][:], cst["misc"]), "sdl")
            k.inc(s.dma_start(sb["maskc"][:], cst["mask_causal"]), "sdl")
            k.inc(s.dma_start(sb["masku"][:], cst["mask_upper"]), "sdl")
            k.inc(s.dma_start(sb["rotT"][:], cst["rotT"]), "sdl")
        k.sec("sync", f_c)

        for ph in phases:
            with ExitStack() as pstack:
                if ph == "ln0":
                    ph_ln0(k, io, sb, pstack)
                elif ph == "l0q":
                    ph_proj_se(k, io, sb, pstack, "l0wq", big2, spill="hbm_q")
                elif ph == "l0v":
                    ph_proj_fm(k, io, sb, pstack, "l0wv", big2, spill="hbm_v0T")
                elif ph == "l0k":
                    ph_proj_se(k, io, sb, pstack, "l0wk", big2, spill=None)
                elif ph == "l0sc":
                    ph_l0_scores(k, io, sb, pstack, big1, big2, sb["xnT"])
                elif ph == "l0av":
                    ph_l0_av(k, io, sb, pstack, sb["xnT"], big1, big2)
                elif ph == "l0ao":
                    wobuf = k.sb([128, 32768], BF16, OFF_BIG0, "wobuf")
                    ph_l0_aout(k, io, sb, pstack, big2, wobuf)
                elif ph == "l0gate":
                    ph_l0gate(k, io, sb, pstack)
                elif ph == "l0k1":
                    ph_proj_se(k, io, sb, pstack, "wqkvT", big2, col0=2048, tag="k1")
                elif ph == "l0selk":
                    ph_l0sel(k, io, sb, pstack, big2, "k_sh", use_rz=False)
                elif ph == "l0kshT":
                    ph_l0ksh_T(k, io, sb, pstack)
                elif ph == "l0v1":
                    ph_proj_se(k, io, sb, pstack, "wqkvT", big2, col0=4096, tag="v1")
                elif ph == "l0selv":
                    ph_l0sel(k, io, sb, pstack, big2, "v_sh", use_rz=True)
                elif ph == "l0q1":
                    ph_proj_fm(k, io, sb, pstack, "wqkvT", big1, col0=0, tag="q1",
                               woff=OFF_BIG2)
                elif ph == "l0fat":
                    ph_l0fat(k, io, sb, pstack, big1)
                elif ph == "ln1":
                    ph_ln1(k, io, sb, pstack, sb["xnT"], big1)
                elif ph == "l1v":
                    ph_l1v(k, io, sb, pstack, sb["xnT"], big2)
                elif ph == "l1at":
                    ph_l1at(k, io, sb, pstack, sb["xnT"], big2)
                elif ph == "l1o":
                    wobuf = k.sb([128, 32768], BF16, OFF_BIG1, "l1wobuf")
                    ybuf = k.sb([128, 32768], BF16, OFF_BIG0, "l1ybuf")
                    ph_l1o(k, io, sb, pstack, ybuf, wobuf)
                elif ph == "wdtr":
                    ph_wdtr(k, io, sb, pstack)
                elif ph == "l1f1":
                    xn1Tb = k.sb([128, 32768], BF16, OFF_BIG1, "f1xn1T")
                    tb = k.sb([128, 32768], BF16, OFF_BIG0, "f1t")
                    ph_l1f1(k, io, sb, pstack, xn1Tb, tb, big2)
                elif ph == "dbgz":
                    end_ = k.snap()
                    def f_dz(s, end_=end_):
                        k.wait_snap(s, end_)
                        k.inc(s.dma_start(io["hbm_yT"], big2[:]), "sds")
                    k.sec("sync", f_dz)
                elif ph == "dbgw2":
                    end_ = k.snap()
                    w2v = k.sb([128, 32768], BF16, OFF_BIG1, "w2view")
                    def f_dw(s, end_=end_, w2v=w2v):
                        k.wait_snap(s, end_)
                        k.inc(s.dma_start(io["hbm_yT"], w2v[:]), "sds")
                    k.sec("sync", f_dw)
                elif ph == "dbgwd":
                    end_ = k.snap()
                    wdv = k.sb([128, 2048], F32, OFF_RS + 256, "wdview")
                    def f_dd(s, end_=end_, wdv=wdv):
                        k.wait_snap(s, end_)
                        k.inc(s.dma_start(io["hbm_x1"][0:128, :], wdv[:]), "sds")
                    k.sec("sync", f_dd)
                elif ph in ("l1f2a", "l1f2b", "l1f2c", "l1f2d"):
                    sp = {"l1f2a": 0, "l1f2b": 1, "l1f2c": 2, "l1f2d": 3}[ph]
                    w2b = k.sb([128, 32768], BF16, OFF_BIG1, f"w2{sp}")
                    state["f2m"] = ph_l1f2(k, io, sb, pstack, big2, w2b, sp,
                                           first=(sp == 0),
                                           prev_m=state.get("f2m"))
                else:
                    raise ValueError(ph)

        end = k.snap()

        def f_end(s):
            k.wait_snap(s, end)
        k.sec("sync", f_end)

    return nc


def _layout_inputs(inputs):
    g = {}
    g["l0wq"] = np.ascontiguousarray(np.asarray(inputs["L0_attn_Wq"], dtype=np.float32))
    g["l0wk"] = np.ascontiguousarray(np.asarray(inputs["L0_attn_Wk"], dtype=np.float32))
    g["l0wv"] = np.ascontiguousarray(np.asarray(inputs["L0_attn_Wv"], dtype=np.float32))
    g["l0wo"] = np.ascontiguousarray(np.asarray(inputs["L0_attn_Wo"], dtype=np.float32))
    g["wgT"] = np.ascontiguousarray(np.asarray(inputs["L0_ffn_W_gate"], dtype=np.float32).T)
    g["wqkvT"] = np.ascontiguousarray(np.asarray(inputs["L0_ffn_W_qkv"], dtype=np.float32).T)
    t = lambda nm: np.ascontiguousarray(np.asarray(inputs[nm], dtype=np.float32).T)
    g["l1wqT"] = t("L1_attn_wq")
    g["l1wkT"] = t("L1_attn_wk")
    g["l1wvT"] = t("L1_attn_wv")
    g["l1woT"] = t("L1_attn_wo")
    g["qAT"] = t("L1_attn_qA")
    g["kAT"] = t("L1_attn_kA")
    g["vAT"] = t("L1_attn_vA")
    g["oAT"] = t("L1_attn_oA")
    g["qBT"] = t("L1_attn_qB")
    g["kBT"] = t("L1_attn_kB")
    g["vBT"] = t("L1_attn_vB")
    g["oBT"] = t("L1_attn_oB")
    col = lambda nm, kk: np.ascontiguousarray(
        np.asarray(inputs[nm], dtype=np.float32).reshape(kk, 128).T)
    g["nw2d"] = col("L1_attn_norm_w", 16)
    g["suug2d"] = col("L1_ffn_SU_ug", 16)
    g["sudp2d"] = col("L1_ffn_SU_dp", 64)
    wsc = np.asarray(inputs["L1_ffn_Wscale_ug"], dtype=np.float32)
    g["wscup2d"] = np.ascontiguousarray(wsc[:8192].reshape(64, 128).T)
    g["wscgate2d"] = np.ascontiguousarray(wsc[8192:].reshape(64, 128).T)
    g["wscdp_row"] = np.ascontiguousarray(
        np.asarray(inputs["L1_ffn_Wscale_dp"], dtype=np.float32).reshape(1, D))
    g["l1wupT"] = t("L1_ffn_W_up")
    g["l1wgateT"] = t("L1_ffn_W_gate")
    g["l1wdT"] = t("L1_ffn_W_down")
    return g


def kernel(**inputs):
    x_full = np.asarray(inputs["L0_attn_x"], dtype=np.float32)
    g = _layout_inputs(inputs)
    results = [dict() for _ in range(N_CORES)]
    carry = [dict() for _ in range(N_CORES)]
    for gi, phases in enumerate(GROUPS):
        io_kinds = {}
        ext = ["x", "l0wq", "l0wk", "l0wv", "l0wo", "wgT", "wqkvT",
               "l1wqT", "l1wkT", "l1wvT", "l1woT", "qAT", "kAT", "vAT", "oAT",
               "qBT", "kBT", "vBT", "oBT", "nw2d", "suug2d", "sudp2d",
               "wscup2d", "wscgate2d", "wscdp_row", "l1wupT", "l1wgateT", "l1wdT"]
        for nm in ext:
            io_kinds[nm] = "ExternalInput"
        io_kinds[FINAL_OUT] = "ExternalOutput"
        for nm in EXPOSE:
            io_kinds[nm] = "ExternalOutput"
        nc = build_group(phases, io_kinds)
        in_maps = []
        for b in range(N_CORES):
            d = {"x": np.ascontiguousarray(x_full[b])}
            d.update(g)
            d.update(carry[b])
            in_maps.append(d)
        res = run_bass_kernel_spmd(nc, in_maps, core_ids=list(range(N_CORES)))
        for b in range(N_CORES):
            results[b].update(res.results[b])
    global _RES
    _RES = results
    out = np.stack([results[b][FINAL_OUT] for b in range(N_CORES)], axis=0)
    return out
